# version 01
# baseline (speedup 1.0000x reference)
import numpy as np
import ml_dtypes
import concourse.bass as bass
import concourse.mybir as mybir
from concourse.bass_utils import run_bass_kernel_spmd

F32 = mybir.dt.float32
BF16 = mybir.dt.bfloat16
I32 = mybir.dt.int32
AF = mybir.ActivationFunctionType
ALU = mybir.AluOpType
AXX = mybir.AxisListType.X

T = 2048
D = 1024
DCH = 8
NT = 4
H0 = 4096
H1 = 3072
NEGBIG = -1.0e9

N_SUB_DEFAULT = 6
DEBUG = {}
_NAME_N = {}


def _uname(name):
    i = _NAME_N.get(name, 0)
    _NAME_N[name] = i + 1
    return f"{name}_{i}"
ENGINES = ("sync", "tensor", "scalar", "vector", "gpsimd")


class _Item:
    __slots__ = ("op", "args", "kw", "incs")

    def __init__(self, op, args, kw):
        self.op, self.args, self.kw = op, args, kw
        self.incs = []


DMA_SEMS = ("qdma", "gdma", "tdma", "odma", "xdma")
NROT = 8


class _EngProxy:
    def __init__(self, sched, name):
        self._s = sched
        self._name = name

    def __getattr__(self, op):
        if op.startswith("_"):
            raise AttributeError(op)

        def call(*args, **kw):
            it = _Item(op, args, kw)
            self._s.q[self._name].append(it)
            return _ItemRef(self._s, it)
        return call

    def wait(self, semname, value=None):
        if value is None:
            semname, value = semname
        it = _Item("wait_ge", (semname, value), {})
        self._s.q[self._name].append(it)


class _ItemRef:
    def __init__(self, sched, item):
        self._s, self._it = sched, item

    def inc(self, semname, k=1):
        p = self._s.p
        if semname in DMA_SEMS:
            j = p.rr.get(semname, 0)
            p.rr[semname] = (j + 1) % NROT
            real = f"{semname}{j}"
            p.sem(real)
            p.c[real] += 16
            self._it.incs.append((real, 16))
            return (real, p.c[real])
        self._it.incs.append((semname, k))
        p.c[semname] += k
        return p.c[semname]


class Sched:

    def __init__(self, p):
        self.p = p
        self.q = {e: [] for e in ENGINES}
        for e in ENGINES:
            setattr(self, e, _EngProxy(self, e))

    def flush(self):
        nc = self.p.nc
        p = self.p
        with nc.Block() as block:
            for name in ENGINES:
                items = self.q[name]
                if not items:
                    continue

                def mk(items=items):
                    def body(eng):
                        for it in items:
                            if it.op == "wait_ge":
                                eng.wait_ge(p.sem(it.args[0]), it.args[1])
                                continue
                            r = getattr(eng, it.op)(*it.args, **it.kw)
                            for semname, k in it.incs:
                                r.then_inc(p.sem(semname), k)
                    return body
                getattr(block, name)(mk())
        self.q = {e: [] for e in ENGINES}


class Prog:
    def __init__(self, nc):
        self.nc = nc
        self.c = {}
        self.sems = {}
        self.rr = {}

    def dma_tokens(self, logical):
        out = []
        for j in range(NROT):
            real = f"{logical}{j}"
            if real in self.c and self.c[real] > 0:
                out.append((real, self.c[real]))
        return out

    def sem(self, name):
        if name not in self.sems:
            self.sems[name] = self.nc.alloc_semaphore(name)
            self.c.setdefault(name, 0)
        return self.sems[name]

    def pre(self, name):
        self.sem(name)
        return self.c[name]


def build_nc(n_sub=N_SUB_DEFAULT):
    _NAME_N.clear()
    nc = bass.Bass("TRN2", target_bir_lowering=False)
    p = Prog(nc)
    for s in ("qdma", "gdma", "tdma", "odma", "mm", "sc", "vx", "gp"):
        p.sem(s)

    dx = nc.dram_tensor("x_fm", [D, T], F32, kind="ExternalInput").ap()
    d_w1n = nc.dram_tensor("w1n", [H0, D], F32, kind="ExternalInput").ap()
    d_w2n = nc.dram_tensor("w2n", [D, H0], F32, kind="ExternalInput").ap()
    d_w1tb = nc.dram_tensor("w1tb", [32, 128, 1024], F32, kind="ExternalInput").ap()
    d_w2tb = nc.dram_tensor("w2tb", [8, 128, 4096], F32, kind="ExternalInput").ap()
    d_b1 = nc.dram_tensor("b1", [1, H0], F32, kind="ExternalInput").ap()
    d_b2 = nc.dram_tensor("b2", [1, D], F32, kind="ExternalInput").ap()
    d_wkqv = nc.dram_tensor("wkqvtb", [24, 128, 1024], F32, kind="ExternalInput").ap()
    d_wproj0 = nc.dram_tensor("wproj0tb", [8, 128, 1024], F32, kind="ExternalInput").ap()
    d_bproj0 = nc.dram_tensor("bproj0", [1, D], F32, kind="ExternalInput").ap()
    d_wqkv1 = nc.dram_tensor("wqkv1tb", [24, 128, 1024], F32, kind="ExternalInput").ap()
    d_wproj1 = nc.dram_tensor("wproj1tb", [8, 128, 1024], F32, kind="ExternalInput").ap()
    d_bproj1 = nc.dram_tensor("bproj1", [1, D], F32, kind="ExternalInput").ap()
    d_wg = nc.dram_tensor("wgtb", [24, 128, 1024], I32, kind="ExternalInput").ap()
    d_wu = nc.dram_tensor("wutb", [24, 128, 1024], I32, kind="ExternalInput").ap()
    d_wd = nc.dram_tensor("wdtb", [8, 128, 3072], I32, kind="ExternalInput").ap()
    d_ident = nc.dram_tensor("ident_bf", [128, 128], BF16, kind="ExternalInput").ap()
    d_identf = nc.dram_tensor("ident_f32", [128, 128], F32, kind="ExternalInput").ap()
    d_mask = nc.dram_tensor("maskneg", [128, 128], F32, kind="ExternalInput").ap()

    dy = nc.dram_tensor("y_fm", [D, T], F32, kind="ExternalOutput").ap()

    s_sign1 = nc.dram_tensor("sign1sc", [32, 128, 1024], BF16).ap()
    s_sign2 = nc.dram_tensor("sign2sc", [8, 128, 4096], BF16).ap()
    s_qkv0 = nc.dram_tensor("qkv0sc", [2, H1, T], BF16).ap()
    s_qkv1 = nc.dram_tensor("qkv1sc", [2, H1, T], BF16).ap()
    s_ctx = nc.dram_tensor("ctxsc", [2, D, T], BF16).ap()
    s_row128 = nc.dram_tensor("row128sc", [128, 1], F32).ap()

    with (
        nc.sbuf_tensor(_uname("x_sb"), [128, DCH * T], F32) as x_sb,
        nc.sbuf_tensor(_uname("ident"), [128, 128], BF16) as ident,
        nc.sbuf_tensor(_uname("identf"), [128, 128], F32) as identf,
        nc.sbuf_tensor(_uname("masksb"), [128, 128], F32) as masksb,
        nc.sbuf_tensor(_uname("onescol_f"), [128, 1], F32) as onescol_f,
        nc.sbuf_tensor(_uname("onesrow_f"), [1, 128], F32) as onesrow_f,
        nc.sbuf_tensor(_uname("onesrow_b"), [1, 128], BF16) as onesrow_b,
        nc.sbuf_tensor(_uname("one_f"), [1, 1], F32) as one_f,
        nc.sbuf_tensor(_uname("alpha1c"), [128, 32], F32) as alpha1c,
        nc.sbuf_tensor(_uname("b1c"), [128, 32], F32) as b1c,
        nc.sbuf_tensor(_uname("ha2c"), [128, 8], F32) as ha2c,
        nc.sbuf_tensor(_uname("hb2c"), [128, 8], F32) as hb2c,
        nc.sbuf_tensor(_uname("bp0c"), [128, 8], F32) as bp0c,
        nc.sbuf_tensor(_uname("bp1c"), [128, 8], F32) as bp1c,
    ):
        cst = dict(x_sb=x_sb, ident=ident, identf=identf, masksb=masksb, onescol_f=onescol_f,
                   onesrow_f=onesrow_f, onesrow_b=onesrow_b, one_f=one_f,
                   alpha1c=alpha1c, b1c=b1c, ha2c=ha2c, hb2c=hb2c,
                   bp0c=bp0c, bp1c=bp1c)

        _prologue(p, cst, dx, d_ident, d_identf, d_mask, d_w1n, d_w2n, d_w1tb,
                  d_w2tb, d_b1, d_b2, d_bproj0, d_bproj1, s_sign1, s_sign2)

        for isub in range(n_sub):
            if isub in (0, 2):
                _sub_ffn_l0(p, cst, s_sign1, s_sign2)
            elif isub in (3, 5):
                _sub_ffn_l1(p, cst, d_wg, d_wu, d_wd)
            elif isub == 1:
                _sub_att0(p, cst, d_wkqv, d_wproj0, s_qkv0, s_ctx, s_row128)
            elif isub == 4:
                _sub_att1(p, cst, d_wqkv1, d_wproj1, s_qkv1, s_ctx)

        S = Sched(p)
        S.sync.dma_start(out=dy.rearrange("(c p) t -> p c t", p=128),
                         in_=x_sb[:]).inc("odma", 16)
        for tok in p.dma_tokens("odma"):
            S.sync.wait(tok)
        S.flush()

    return nc


def _prologue(p, cst, dx, d_ident, d_identf, d_mask, d_w1n, d_w2n, d_w1tb,
              d_w2tb, d_b1, d_b2, d_bp0, d_bp1, s_sign1, s_sign2):
    nc = p.nc
    C = p.c
    with (
        nc.sbuf_tensor(_uname("pl_wbuf"), [128, 2 * 4096], F32) as wbuf,
        nc.sbuf_tensor(_uname("pl_sgn"), [128, 2 * 4096], BF16) as sg,
        nc.sbuf_tensor(_uname("pl_rows"), [1, H0 + 3 * D], F32) as rows,
        nc.psum_tensor(_uname("pl_ps"), [128, 32], F32) as ps32,
    ):
        OB1, OB2, OBP0, OBP1 = 0, H0, H0 + D, H0 + 2 * D
        S = Sched(p)
        S.sync.dma_start(out=cst["x_sb"][:],
                         in_=dx.rearrange("(c p) t -> p c t", p=128)).inc("qdma", 16)
        S.sync.dma_start(out=cst["ident"][:], in_=d_ident).inc("qdma", 16)
        S.sync.dma_start(out=cst["identf"][:], in_=d_identf).inc("qdma", 16)
        S.sync.dma_start(out=cst["masksb"][:], in_=d_mask).inc("qdma", 16)
        S.sync.dma_start(out=rows[:, OB1:OB1 + H0], in_=d_b1).inc("qdma", 16)
        S.sync.dma_start(out=rows[:, OB2:OB2 + D], in_=d_b2).inc("qdma", 16)
        S.sync.dma_start(out=rows[:, OBP0:OBP0 + D], in_=d_bp0).inc("qdma", 16)
        nrows = S.sync.dma_start(out=rows[:, OBP1:OBP1 + D], in_=d_bp1).inc("qdma", 16)
        S.gpsimd.memset(cst["onescol_f"][:], 1.0)
        S.gpsimd.memset(cst["onesrow_f"][:], 1.0)
        S.gpsimd.memset(cst["onesrow_b"][:], 1.0)
        ng = S.gpsimd.memset(cst["one_f"][:], 1.0).inc("gp", 1)

        S.tensor.wait(nrows)
        S.tensor.wait("gp", ng)
        for i in range(32):
            r = S.tensor.matmul(ps32[:, i:i + 1],
                                rows[:, OB1 + i * 128:OB1 + (i + 1) * 128],
                                cst["one_f"][:], start=True, stop=True)
        nmm = r.inc("mm", 1)
        S.scalar.wait("mm", nmm)
        nsc = S.scalar.copy(cst["b1c"][:], ps32[:]).inc("sc", 1)
        S.tensor.wait("sc", nsc)
        for i in range(8):
            S.tensor.matmul(ps32[:, i:i + 1],
                            rows[:, OB2 + i * 128:OB2 + (i + 1) * 128],
                            cst["one_f"][:], start=True, stop=True)
        for i in range(8):
            S.tensor.matmul(ps32[:, 8 + i:9 + i],
                            rows[:, OBP0 + i * 128:OBP0 + (i + 1) * 128],
                            cst["one_f"][:], start=True, stop=True)
        for i in range(8):
            r = S.tensor.matmul(ps32[:, 16 + i:17 + i],
                                rows[:, OBP1 + i * 128:OBP1 + (i + 1) * 128],
                                cst["one_f"][:], start=True, stop=True)
        nmm = r.inc("mm", 1)
        S.scalar.wait("mm", nmm)
        S.scalar.mul(cst["hb2c"][:], ps32[:, 0:8], 0.5)
        S.scalar.copy(cst["bp0c"][:], ps32[:, 8:16])
        S.scalar.copy(cst["bp1c"][:], ps32[:, 16:24]).inc("sc", 1)

        mark_red = {}
        for i in range(40):
            if i >= 2:
                S.sync.wait("vx", mark_red[i - 2])
            if i < 32:
                src = d_w1n[i * 128:(i + 1) * 128, :]
                wslice = wbuf[:, (i % 2) * 4096:(i % 2) * 4096 + 1024]
            else:
                src = d_w2n[(i - 32) * 128:(i - 31) * 128, :]
                wslice = wbuf[:, (i % 2) * 4096:(i % 2) * 4096 + 4096]
            nq = S.sync.dma_start(out=wslice, in_=src).inc("qdma", 16)
            S.vector.wait(nq)
            if i < 32:
                r = S.vector.reduce_sum(cst["alpha1c"][:, i:i + 1], wslice,
                                        axis=AXX, apply_absolute_value=True)
            else:
                r = S.vector.reduce_sum(cst["ha2c"][:, i - 32:i - 31], wslice,
                                        axis=AXX, apply_absolute_value=True)
            mark_red[i] = r.inc("vx", 1)
        nred = S.vector.engine_nop().inc("vx", 1)
        S.scalar.wait("vx", nred)
        S.scalar.mul(cst["alpha1c"][:], cst["alpha1c"][:], 1.0 / 1024.0)
        nsc_a = S.scalar.mul(cst["ha2c"][:], cst["ha2c"][:], 0.5 / 4096.0).inc("sc", 1)
        nvx_a = nred
        S.sync.wait("sc", nsc_a)

        S.sync.wait("vx", nvx_a)
        mark_sgn = {}
        for i in range(40):
            if i >= 2:
                S.sync.wait(mark_sgn[i - 2])
            if i < 32:
                src = d_w1tb[i]
                n = 1024
            else:
                src = d_w2tb[i - 32]
                n = 4096
            nq = S.sync.dma_start(out=wbuf[:, (i % 2) * 4096:(i % 2) * 4096 + n],
                                  in_=src).inc("qdma", 16)
            S.scalar.wait(nq)
            if i >= 2:
                S.scalar.wait(mark_sgn[i - 2])
            nsg = S.scalar.sign(sg[:, (i % 2) * 4096:(i % 2) * 4096 + n],
                                wbuf[:, (i % 2) * 4096:(i % 2) * 4096 + n]).inc("sc", 1)
            dst = s_sign1[i] if i < 32 else s_sign2[i - 32]
            S.sync.wait("sc", nsg)
            mark_sgn[i] = S.sync.dma_start(
                out=dst, in_=sg[:, (i % 2) * 4096:(i % 2) * 4096 + n]).inc("tdma", 16)
        S.gpsimd.wait(mark_sgn[39])
        S.flush()


from contextlib import ExitStack


def _ln_scratch(nc, stack):
    scr = {}
    scr["sqb"] = stack.enter_context(nc.sbuf_tensor(_uname("ln_sq"), [128, 2 * 512], F32))
    scr["abc"] = stack.enter_context(nc.sbuf_tensor(_uname("ln_abc"), [128, 2 * 512], F32))
    scr["bbc"] = stack.enter_context(nc.sbuf_tensor(_uname("ln_bbc"), [128, 2 * 512], F32))
    scr["l32"] = stack.enter_context(nc.sbuf_tensor(_uname("ln_l32"), [128, 2 * 512], F32))
    scr["rw"] = stack.enter_context(nc.sbuf_tensor(_uname("ln_rows"), [1, 4 * 512], F32))
    scr["ps_s"] = stack.enter_context(nc.psum_tensor(_uname("ln_ps_s"), [1, 512], F32))
    scr["ps_q"] = stack.enter_context(nc.psum_tensor(_uname("ln_ps_q"), [1, 512], F32))
    scr["ps_a"] = stack.enter_context(nc.psum_tensor(_uname("ln_ps_a"), [128, 512], F32))
    scr["ps_b"] = stack.enter_context(nc.psum_tensor(_uname("ln_ps_b"), [128, 512], F32))
    scr["m_stats"] = {}; scr["m_rows"] = {}; scr["m_sqrt"] = {}
    scr["m_rows2"] = {}; scr["m_bc"] = {}; scr["m_evbc"] = {}
    scr["m_norm"] = {}; scr["m_split"] = {}; scr["m_sq"] = {}; scr["m_sumq"] = {}
    scr["seq"] = 0
    return scr


def _emit_ln_tile(p, S, cst, t, scr, lnslice, gate_vx_mm=None):
    x_sb = cst["x_sb"]
    sqb, abc, bbc, l32, rw = scr["sqb"], scr["abc"], scr["bbc"], scr["l32"], scr["rw"]
    ps_s, ps_q, ps_a, ps_b = scr["ps_s"], scr["ps_q"], scr["ps_a"], scr["ps_b"]
    m_stats, m_rows, m_sqrt = scr["m_stats"], scr["m_rows"], scr["m_sqrt"]
    m_rows2, m_bc, m_evbc = scr["m_rows2"], scr["m_bc"], scr["m_evbc"]
    m_norm, m_split = scr["m_norm"], scr["m_split"]
    m_sq, m_sumq = scr["m_sq"], scr["m_sumq"]
    q = scr["seq"]; scr["seq"] = q + 1
    s = 512 * t

    if q >= 1:
        S.tensor.wait("vx", m_rows[q - 1])
    for c in range(DCH):
        gq = q * DCH + c
        if gq >= 2:
            S.scalar.wait("mm", m_sumq[gq - 2])
        m_sq[gq] = S.scalar.square(sqb[:, (gq % 2) * 512:(gq % 2) * 512 + 512],
                                   x_sb[:, c * T + s:c * T + s + 512]).inc("sc", 1)
        S.tensor.matmul(ps_s[:], cst["onescol_f"][:],
                        x_sb[:, c * T + s:c * T + s + 512],
                        start=(c == 0), stop=(c == DCH - 1))
        S.tensor.wait("sc", m_sq[gq])
        r = S.tensor.matmul(ps_q[:], cst["onescol_f"][:],
                            sqb[:, (gq % 2) * 512:(gq % 2) * 512 + 512],
                            start=(c == 0), stop=(c == DCH - 1))
        m_sumq[gq] = r.inc("mm", 1)
    m_stats[q] = m_sumq[q * DCH + DCH - 1]
    R0, R1, R2 = rw[:, 0:512], rw[:, 512:1024], rw[:, 1024:1536]
    R3, Ra, Rb = rw[:, 1024:1536], rw[:, 1536:2048], rw[:, 512:1024]
    S.vector.wait("mm", m_stats[q])
    S.vector.tensor_scalar_mul(R0, ps_s[:], 1.0 / D)
    S.vector.tensor_scalar_mul(R1, ps_q[:], 1.0 / D)
    S.vector.tensor_mul(R2, R0, R0)
    S.vector.tensor_sub(R1, R1, R2)
    S.vector.tensor_scalar_add(R1, R1, 1.0e-5)
    m_rows[q] = S.vector.reciprocal(R2, R1).inc("vx", 1)
    S.scalar.wait("vx", m_rows[q])
    m_sqrt[q] = S.scalar.activation(Ra, R2, AF.Sqrt).inc("sc", 1)
    S.vector.wait("sc", m_sqrt[q])
    S.vector.tensor_mul(R3, Ra, Ra)
    S.vector.tensor_mul(R3, R3, R1)
    S.vector.tensor_scalar(R3, R3, -0.5, 1.5, ALU.mult, ALU.add)
    S.vector.tensor_mul(Ra, Ra, R3)
    m_rows2[q] = S.vector.tensor_mul(Rb, R0, Ra).inc("vx", 1)
    if q >= 1:
        S.tensor.wait("sc", m_evbc[q - 1])
    S.tensor.wait("vx", m_rows2[q])
    S.tensor.matmul(ps_a[:], cst["onesrow_f"][:], Ra, start=True, stop=True)
    r = S.tensor.matmul(ps_b[:], cst["onesrow_f"][:], Rb, start=True, stop=True)
    m_bc[q] = r.inc("mm", 1)
    S.scalar.wait("mm", m_bc[q])
    if q >= 2:
        S.scalar.wait("gp", m_norm[(q - 2, DCH - 1)])
    S.scalar.copy(abc[:, (q % 2) * 512:(q % 2) * 512 + 512], ps_a[:])
    m_evbc[q] = S.scalar.copy(bbc[:, (q % 2) * 512:(q % 2) * 512 + 512],
                              ps_b[:]).inc("sc", 1)
    S.gpsimd.wait("sc", m_evbc[q])
    first_split = True
    for c in range(DCH):
        gidx = q * DCH + c
        if gidx >= 2:
            S.gpsimd.wait("vx", m_split[gidx - 2])
        lt = l32[:, (gidx % 2) * 512:(gidx % 2) * 512 + 512]
        S.gpsimd.tensor_tensor(lt, x_sb[:, c * T + s:c * T + s + 512],
                               abc[:, (q % 2) * 512:(q % 2) * 512 + 512], ALU.mult)
        m_norm[(q, c)] = S.gpsimd.tensor_tensor(
            lt, lt, bbc[:, (q % 2) * 512:(q % 2) * 512 + 512],
            ALU.subtract).inc("gp", 1)
        S.vector.wait("gp", m_norm[(q, c)])
        if first_split and gate_vx_mm is not None:
            S.vector.wait("mm", gate_vx_mm)
            first_split = False
        hi_ap, lo_ap = lnslice(c)
        S.vector.tensor_copy(hi_ap, lt)
        m_split[gidx] = S.vector.tensor_tensor(lo_ap, lt, hi_ap,
                                               ALU.subtract).inc("vx", 1)
    return m_split[q * DCH + DCH - 1]


def _sub_ffn_l0(p, cst, s_sign1, s_sign2):
    nc = p.nc
    x_sb = cst["x_sb"]
    with ExitStack() as st:
        ln_hi = st.enter_context(nc.sbuf_tensor(_uname("f0_lnh"), [128, 2 * DCH * 512], BF16))
        ln_lo = st.enter_context(nc.sbuf_tensor(_uname("f0_lnl"), [128, 2 * DCH * 512], BF16))
        h_hi = st.enter_context(nc.sbuf_tensor(_uname("f0_hh"), [128, 32 * 512], BF16))
        h_lo = st.enter_context(nc.sbuf_tensor(_uname("f0_hl"), [128, 32 * 512], BF16))
        s1b = st.enter_context(nc.sbuf_tensor(_uname("f0_s1"), [128, 2 * 1024], BF16))
        s2b = st.enter_context(nc.sbuf_tensor(_uname("f0_s2"), [128, 2 * 2048], BF16))
        g32 = st.enter_context(nc.sbuf_tensor(_uname("f0_g32"), [128, 2 * 512], F32))
        r32 = st.enter_context(nc.sbuf_tensor(_uname("f0_r32"), [128, 2 * 512], F32))
        p1a = st.enter_context(nc.psum_tensor(_uname("f0_p1a"), [128, 512], F32))
        p1b = st.enter_context(nc.psum_tensor(_uname("f0_p1b"), [128, 512], F32))
        p2a = st.enter_context(nc.psum_tensor(_uname("f0_p2a"), [128, 512], F32))
        p2b = st.enter_context(nc.psum_tensor(_uname("f0_p2b"), [128, 512], F32))
        scr = _ln_scratch(nc, st)
        S = Sched(p)
        p1 = (p1a, p1b)
        p2 = (p2a, p2b)
        m_w1 = {}; m_mm1 = {}; m_gelu = {}; m_hsplit = {}
        m_w2 = {}; m_mm2 = {}; m_ev2 = {}; m_res = {}; m_half = {}
        for t in range(NT):
            s = 512 * t
            lb = (t % 2) * DCH * 512

            def lnslice(c, lb=lb):
                return (ln_hi[:, lb + c * 512:lb + (c + 1) * 512],
                        ln_lo[:, lb + c * 512:lb + (c + 1) * 512])
            gate = m_mm1[(t - 2) * 32 + 31] if t >= 2 else None
            _emit_ln_tile(p, S, cst, t, scr, lnslice, gate_vx_mm=gate)
            for m in range(32):
                g = t * 32 + m
                if g >= 2:
                    S.sync.wait("mm", m_mm1[g - 2])
                m_w1[g] = S.sync.dma_start(
                    out=s1b[:, (g % 2) * 1024:(g % 2) * 1024 + 1024],
                    in_=s_sign1[m]).inc("qdma", 16)
                S.tensor.wait(m_w1[g])
                if m == 0:
                    S.tensor.wait("vx", scr["m_split"][t * DCH + DCH - 1])
                if g >= 2:
                    S.tensor.wait("sc", m_gelu[g - 2])
                for c in range(DCH):
                    for hl, ln in ((0, ln_hi), (1, ln_lo)):
                        r = S.tensor.matmul(
                            p1[g % 2][:],
                            s1b[:, (g % 2) * 1024 + c * 128:(g % 2) * 1024 + (c + 1) * 128],
                            ln[:, lb + c * 512:lb + (c + 1) * 512],
                            start=(c == 0 and hl == 0),
                            stop=(c == DCH - 1 and hl == 1))
                m_mm1[g] = r.inc("mm", 1)
                S.scalar.wait("mm", m_mm1[g])
                if g >= 2:
                    S.scalar.wait("vx", m_hsplit[g - 2])
                m_gelu[g] = S.scalar.activation(
                    g32[:, (g % 2) * 512:(g % 2) * 512 + 512], p1[g % 2][:],
                    AF.Gelu, bias=cst["b1c"][:, m:m + 1],
                    scale=cst["alpha1c"][:, m:m + 1]).inc("sc", 1)
                S.vector.wait("sc", m_gelu[g])
                if t >= 1 and m == 0:
                    S.vector.wait("mm", m_mm2[(t - 1, 7)])
                gs = g32[:, (g % 2) * 512:(g % 2) * 512 + 512]
                S.vector.tensor_copy(h_hi[:, m * 512:(m + 1) * 512], gs)
                m_hsplit[g] = S.vector.tensor_tensor(
                    h_lo[:, m * 512:(m + 1) * 512], gs,
                    h_hi[:, m * 512:(m + 1) * 512], ALU.subtract).inc("vx", 1)
            for m2 in range(8):
                g2 = t * 8 + m2
                for hf in range(2):
                    gh = g2 * 2 + hf
                    if gh >= 2:
                        S.sync.wait("mm", m_half[gh - 2])
                    m_w2[gh] = S.sync.dma_start(
                        out=s2b[:, (gh % 2) * 2048:(gh % 2) * 2048 + 2048],
                        in_=s_sign2[m2][:, hf * 2048:(hf + 1) * 2048]).inc("qdma", 16)
                S.tensor.wait("vx", m_hsplit[t * 32 + 31])
                if g2 >= 2:
                    S.tensor.wait("sc", m_ev2[(t, m2 - 2)] if m2 >= 2
                                  else m_ev2[(t - 1, m2 + 6)])
                for hf in range(2):
                    gh = g2 * 2 + hf
                    S.tensor.wait(m_w2[gh])
                    for kk in range(16):
                        k2 = hf * 16 + kk
                        for hl, hh in ((0, h_hi), (1, h_lo)):
                            r = S.tensor.matmul(
                                p2[g2 % 2][:],
                                s2b[:, (gh % 2) * 2048 + kk * 128:(gh % 2) * 2048 + (kk + 1) * 128],
                                hh[:, k2 * 512:(k2 + 1) * 512],
                                start=(k2 == 0 and hl == 0),
                                stop=(k2 == 31 and hl == 1))
                    m_half[gh] = r.inc("mm", 1)
                m_mm2[(t, m2)] = m_half[g2 * 2 + 1]
                S.scalar.wait("mm", m_mm2[(t, m2)])
                if g2 >= 2:
                    S.scalar.wait("vx", m_res[(t, m2 - 2)] if m2 >= 2
                                  else m_res[(t - 1, m2 + 6)])
                m_ev2[(t, m2)] = S.scalar.activation(
                    r32[:, (g2 % 2) * 512:(g2 % 2) * 512 + 512], p2[g2 % 2][:],
                    AF.Identity, bias=cst["hb2c"][:, m2:m2 + 1],
                    scale=cst["ha2c"][:, m2:m2 + 1]).inc("sc", 1)
                S.vector.wait("sc", m_ev2[(t, m2)])
                xs = x_sb[:, m2 * T + s:m2 * T + s + 512]
                m_res[(t, m2)] = S.vector.tensor_add(
                    xs, xs, r32[:, (g2 % 2) * 512:(g2 % 2) * 512 + 512]).inc("vx", 1)
        if "ffn0_h" in DEBUG:
            o_hh, o_hl, o_lnh2, o_lnl2, o_ha2, o_hb2, o_r32 = DEBUG["ffn0_h"]
            S.sync.wait("vx", p.c["vx"])
            S.sync.wait("mm", p.c["mm"])
            S.sync.wait("sc", p.c["sc"])
            S.sync.dma_start(out=o_hh, in_=h_hi[:]).inc("odma", 16)
            S.sync.dma_start(out=o_hl, in_=h_lo[:]).inc("odma", 16)
            S.sync.dma_start(out=o_lnh2, in_=ln_hi[:]).inc("odma", 16)
            S.sync.dma_start(out=o_lnl2, in_=ln_lo[:]).inc("odma", 16)
            S.sync.dma_start(out=o_ha2, in_=cst["ha2c"][:]).inc("odma", 16)
            S.sync.dma_start(out=o_hb2, in_=cst["hb2c"][:]).inc("odma", 16)
            S.sync.dma_start(out=o_r32, in_=r32[:]).inc("odma", 16)
            if "ffn0_praw" in DEBUG:
                o_praw = DEBUG.pop("ffn0_praw")
                npr = S.scalar.copy(g32[:, 0:512], p2[1][:]).inc("sc", 1)
                S.sync.wait("sc", npr)
                S.sync.dma_start(out=o_praw, in_=g32[:, 0:512]).inc("odma", 16)
            for tok in p.dma_tokens("odma"):
                S.sync.wait(tok)
            DEBUG.pop("ffn0_h")
        S.flush()


def _sub_ffn_l1(p, cst, d_wg, d_wu, d_wd):
    nc = p.nc
    x_sb = cst["x_sb"]
    with ExitStack() as st:
        ln_hi = st.enter_context(nc.sbuf_tensor(_uname("f1_lnh"), [128, 2 * DCH * 512], BF16))
        ln_lo = st.enter_context(nc.sbuf_tensor(_uname("f1_lnl"), [128, 2 * DCH * 512], BF16))
        h_hi = st.enter_context(nc.sbuf_tensor(_uname("f1_hh"), [128, 24 * 512], BF16))
        h_lo = st.enter_context(nc.sbuf_tensor(_uname("f1_hl"), [128, 24 * 512], BF16))
        wgb = st.enter_context(nc.sbuf_tensor(_uname("f1_wg"), [128, 2 * 1024], BF16))
        wub = st.enter_context(nc.sbuf_tensor(_uname("f1_wu"), [128, 2 * 1024], BF16))
        wdb = st.enter_context(nc.sbuf_tensor(_uname("f1_wd"), [128, 2 * 3072], BF16))
        g32 = st.enter_context(nc.sbuf_tensor(_uname("f1_g32"), [128, 2 * 512], F32))
        h32 = st.enter_context(nc.sbuf_tensor(_uname("f1_h32"), [128, 2 * 512], F32))
        pga = st.enter_context(nc.psum_tensor(_uname("f1_pga"), [128, 512], F32))
        pgb = st.enter_context(nc.psum_tensor(_uname("f1_pgb"), [128, 512], F32))
        pua = st.enter_context(nc.psum_tensor(_uname("f1_pua"), [128, 512], F32))
        pda = st.enter_context(nc.psum_tensor(_uname("f1_pda"), [128, 512], F32))
        scr = _ln_scratch(nc, st)
        S = Sched(p)
        pg = (pga, pgb); pu = (pua, pua); pd = (pda, pda)
        m_mmg = {}; m_silu = {}; m_h32 = {}; m_hsplit = {}
        m_wd = {}; m_mmd = {}; m_res = {}
        for t in range(NT):
            s = 512 * t
            lb = (t % 2) * DCH * 512

            def lnslice(c, lb=lb):
                return (ln_hi[:, lb + c * 512:lb + (c + 1) * 512],
                        ln_lo[:, lb + c * 512:lb + (c + 1) * 512])
            gate = m_mmg[(t - 2) * 24 + 23] if t >= 2 else None
            _emit_ln_tile(p, S, cst, t, scr, lnslice, gate_vx_mm=gate)
            for m in range(24):
                g = t * 24 + m
                if g >= 2:
                    S.gpsimd.wait("mm", m_mmg[g - 2])
                S.gpsimd.dma_start(
                    out=wgb[:, (g % 2) * 1024:(g % 2) * 1024 + 1024],
                    in_=d_wg[m]).inc("gdma", 16)
                m_wg2 = S.gpsimd.dma_start(
                    out=wub[:, (g % 2) * 1024:(g % 2) * 1024 + 1024],
                    in_=d_wu[m]).inc("gdma", 16)
                S.tensor.wait(m_wg2)
                if m == 0:
                    S.tensor.wait("vx", scr["m_split"][t * DCH + DCH - 1])
                if g >= 2:
                    S.tensor.wait("sc", m_silu[g - 2])
                if g >= 1:
                    S.tensor.wait("vx", m_h32[g - 1])
                for c in range(DCH):
                    for hl, ln in ((0, ln_hi), (1, ln_lo)):
                        S.tensor.matmul(
                            pg[g % 2][:],
                            wgb[:, (g % 2) * 1024 + c * 128:(g % 2) * 1024 + (c + 1) * 128],
                            ln[:, lb + c * 512:lb + (c + 1) * 512],
                            start=(c == 0 and hl == 0),
                            stop=(c == DCH - 1 and hl == 1))
                for c in range(DCH):
                    for hl, ln in ((0, ln_hi), (1, ln_lo)):
                        r = S.tensor.matmul(
                            pu[g % 2][:],
                            wub[:, (g % 2) * 1024 + c * 128:(g % 2) * 1024 + (c + 1) * 128],
                            ln[:, lb + c * 512:lb + (c + 1) * 512],
                            start=(c == 0 and hl == 0),
                            stop=(c == DCH - 1 and hl == 1))
                m_mmg[g] = r.inc("mm", 1)
                S.scalar.wait("mm", m_mmg[g])
                if g >= 2:
                    S.scalar.wait("vx", m_h32[g - 2])
                m_silu[g] = S.scalar.activation(
                    g32[:, (g % 2) * 512:(g % 2) * 512 + 512], pg[g % 2][:],
                    AF.Silu, scale=1.0 / 127.0).inc("sc", 1)
                S.vector.wait("sc", m_silu[g])
                if t >= 1 and m == 0:
                    S.vector.wait("mm", m_mmd[(t - 1, 7)])
                hs = h32[:, (g % 2) * 512:(g % 2) * 512 + 512]
                m_h32[g] = S.vector.tensor_tensor(
                    hs, g32[:, (g % 2) * 512:(g % 2) * 512 + 512],
                    pu[g % 2][:], ALU.mult).inc("vx", 1)
                S.scalar.wait("vx", m_h32[g])
                nhi = S.scalar.copy(h_hi[:, m * 512:(m + 1) * 512], hs).inc("sc", 1)
                S.vector.wait("sc", nhi)
                m_hsplit[g] = S.vector.tensor_tensor(
                    h_lo[:, m * 512:(m + 1) * 512], hs,
                    h_hi[:, m * 512:(m + 1) * 512], ALU.subtract).inc("vx", 1)
            for m2 in range(8):
                g2 = t * 8 + m2
                if g2 >= 2:
                    S.gpsimd.wait("mm", m_mmd[(t, m2 - 2)] if m2 >= 2
                                  else m_mmd[(t - 1, m2 + 6)])
                m_wd[g2] = S.gpsimd.dma_start(
                    out=wdb[:, (g2 % 2) * 3072:(g2 % 2) * 3072 + 3072],
                    in_=d_wd[m2]).inc("gdma", 16)
                S.tensor.wait(m_wd[g2])
                S.tensor.wait("vx", m_hsplit[t * 24 + 23])
                if g2 >= 1:
                    S.tensor.wait("vx", m_res[(t, m2 - 1)] if m2 >= 1
                                  else m_res[(t - 1, 7)])
                for k2 in range(24):
                    for hl, hh in ((0, h_hi), (1, h_lo)):
                        r = S.tensor.matmul(
                            pd[g2 % 2][:],
                            wdb[:, (g2 % 2) * 3072 + k2 * 128:(g2 % 2) * 3072 + (k2 + 1) * 128],
                            hh[:, k2 * 512:(k2 + 1) * 512],
                            start=(k2 == 0 and hl == 0),
                            stop=(k2 == 23 and hl == 1))
                m_mmd[(t, m2)] = r.inc("mm", 1)
                S.vector.wait("mm", m_mmd[(t, m2)])
                xs = x_sb[:, m2 * T + s:m2 * T + s + 512]
                m_res[(t, m2)] = S.vector.scalar_tensor_tensor(
                    xs, pd[g2 % 2][:], 0.5 / (127.0 * 127.0), xs,
                    ALU.mult, ALU.add).inc("vx", 1)
        S.flush()


def _att_phaseA(p, cst, d_wq, s_qkv, relu_rows=0):
    nc = p.nc
    with ExitStack() as st:
        ln_hi = st.enter_context(nc.sbuf_tensor(_uname("aA_lnh"), [128, 2 * DCH * 512], BF16))
        ln_lo = st.enter_context(nc.sbuf_tensor(_uname("aA_lnl"), [128, 2 * DCH * 512], BF16))
        wkq = st.enter_context(nc.sbuf_tensor(_uname("aA_wkq"), [128, 2 * 1024], F32))
        whi = st.enter_context(nc.sbuf_tensor(_uname("aA_whi"), [128, 2 * 1024], BF16))
        wlo = st.enter_context(nc.sbuf_tensor(_uname("aA_wlo"), [128, 2 * 1024], BF16))
        q32 = st.enter_context(nc.sbuf_tensor(_uname("aA_q32"), [128, 2 * 512], F32))
        qhi = st.enter_context(nc.sbuf_tensor(_uname("aA_qhi"), [128, 2 * 512], BF16))
        qlo = st.enter_context(nc.sbuf_tensor(_uname("aA_qlo"), [128, 2 * 512], BF16))
        pka = st.enter_context(nc.psum_tensor(_uname("aA_pka"), [128, 512], F32))
        pkb = st.enter_context(nc.psum_tensor(_uname("aA_pkb"), [128, 512], F32))
        scr = _ln_scratch(nc, st)
        S = Sched(p)
        pk = (pka, pkb)
        m_wld = {}; m_wsp = {}; m_mm = {}; m_ev = {}; m_spl = {}; m_st = {}
        for t in range(NT):
            lb = (t % 2) * DCH * 512

            def lnslice(c, lb=lb):
                return (ln_hi[:, lb + c * 512:lb + (c + 1) * 512],
                        ln_lo[:, lb + c * 512:lb + (c + 1) * 512])
            gate = m_mm[(t - 2, 23)] if t >= 2 else None
            _emit_ln_tile(p, S, cst, t, scr, lnslice, gate_vx_mm=gate)
            for m in range(24):
                g = t * 24 + m
                if g >= 2:
                    S.sync.wait("mm", m_mm[(t, m - 2)] if m >= 2
                                else m_mm[(t - 1, 22 + m)])
                m_wld[g] = S.sync.dma_start(
                    out=wkq[:, (g % 2) * 1024:(g % 2) * 1024 + 1024],
                    in_=d_wq[m]).inc("qdma", 16)
                S.vector.wait(m_wld[g])
                if g >= 2:
                    S.vector.wait("mm", m_mm[(t, m - 2)] if m >= 2
                                  else m_mm[(t - 1, 22 + m)])
                ws = slice((g % 2) * 1024, (g % 2) * 1024 + 1024)
                S.vector.tensor_copy(whi[:, ws], wkq[:, ws])
                m_wsp[g] = S.vector.tensor_tensor(wlo[:, ws], wkq[:, ws],
                                                  whi[:, ws], ALU.subtract).inc("vx", 1)
                if m == 0:
                    S.tensor.wait("vx", scr["m_split"][t * DCH + DCH - 1])
                S.tensor.wait("vx", m_wsp[g])
                if g >= 2:
                    S.tensor.wait("sc", m_ev[(t, m - 2)] if m >= 2
                                  else m_ev[(t - 1, 22 + m)])
                for c in range(DCH):
                    co = (g % 2) * 1024 + c * 128
                    for wop, lnop in ((whi, ln_hi), (wlo, ln_hi), (whi, ln_lo)):
                        r = S.tensor.matmul(
                            pk[g % 2][:], wop[:, co:co + 128],
                            lnop[:, (t % 2) * DCH * 512 + c * 512:(t % 2) * DCH * 512 + (c + 1) * 512],
                            start=(c == 0 and wop is whi and lnop is ln_hi),
                            stop=(c == DCH - 1 and lnop is ln_lo))
                m_mm[(t, m)] = r.inc("mm", 1)
                S.scalar.wait("mm", m_mm[(t, m)])
                if g >= 2:
                    S.scalar.wait("vx", m_spl[(t, m - 2)] if m >= 2
                                  else m_spl[(t - 1, 22 + m)])
                func = AF.Relu if m < relu_rows else AF.Identity
                m_ev[(t, m)] = S.scalar.activation(
                    q32[:, (g % 2) * 512:(g % 2) * 512 + 512], pk[g % 2][:],
                    func).inc("sc", 1)
                S.vector.wait("sc", m_ev[(t, m)])
                if g >= 2:
                    prev = m_st[(t, m - 2)] if m >= 2 else m_st[(t - 1, 22 + m)]
                    S.vector.wait(prev[0])
                    S.vector.wait(prev[1])
                qs = q32[:, (g % 2) * 512:(g % 2) * 512 + 512]
                S.vector.tensor_copy(qhi[:, (g % 2) * 512:(g % 2) * 512 + 512], qs)
                m_spl[(t, m)] = S.vector.tensor_tensor(
                    qlo[:, (g % 2) * 512:(g % 2) * 512 + 512], qs,
                    qhi[:, (g % 2) * 512:(g % 2) * 512 + 512], ALU.subtract).inc("vx", 1)
                S.sync.wait("vx", m_spl[(t, m)])
                st0 = S.sync.dma_start(
                    out=s_qkv[0][m * 128:(m + 1) * 128, t * 512:(t + 1) * 512],
                    in_=qhi[:, (g % 2) * 512:(g % 2) * 512 + 512]).inc("tdma", 16)
                st1 = S.sync.dma_start(
                    out=s_qkv[1][m * 128:(m + 1) * 128, t * 512:(t + 1) * 512],
                    in_=qlo[:, (g % 2) * 512:(g % 2) * 512 + 512]).inc("tdma", 16)
                m_st[(t, m)] = (st0, st1)
        for tok in p.dma_tokens("tdma"):
            S.gpsimd.wait(tok)
        S.flush()


def _att_phaseC(p, cst, d_wp, bp_col, s_ctx):
    nc = p.nc
    x_sb = cst["x_sb"]
    with ExitStack() as st:
        chi = st.enter_context(nc.sbuf_tensor(_uname("aC_chi"), [128, DCH * T], BF16))
        clo = st.enter_context(nc.sbuf_tensor(_uname("aC_clo"), [128, DCH * T], BF16))
        wpq = st.enter_context(nc.sbuf_tensor(_uname("aC_wpq"), [128, 2 * 1024], F32))
        whi = st.enter_context(nc.sbuf_tensor(_uname("aC_whi"), [128, 2 * 1024], BF16))
        wlo = st.enter_context(nc.sbuf_tensor(_uname("aC_wlo"), [128, 2 * 1024], BF16))
        p32 = st.enter_context(nc.sbuf_tensor(_uname("aC_p32"), [128, 2 * 512], F32))
        ppa = st.enter_context(nc.psum_tensor(_uname("aC_ppa"), [128, 512], F32))
        ppb = st.enter_context(nc.psum_tensor(_uname("aC_ppb"), [128, 512], F32))
        S = Sched(p)
        pp = (ppa, ppb)
        lds = []
        for c in range(DCH):
            lds.append(S.sync.dma_start(
                out=chi[:, c * T:(c + 1) * T],
                in_=s_ctx[0][c * 128:(c + 1) * 128, :]).inc("qdma", 16))
            lds.append(S.sync.dma_start(
                out=clo[:, c * T:(c + 1) * T],
                in_=s_ctx[1][c * 128:(c + 1) * 128, :]).inc("qdma", 16))
        m_wld = {}; m_wsp = {}; m_mm = {}; m_ev = {}; m_res = {}
        for m in range(8):
            if m >= 2:
                S.sync.wait("vx", m_wsp[m - 2])
            m_wld[m] = S.sync.dma_start(
                out=wpq[:, (m % 2) * 1024:(m % 2) * 1024 + 1024],
                in_=d_wp[m]).inc("qdma", 16)
            S.vector.wait(m_wld[m])
            if m >= 2:
                S.vector.wait("mm", m_mm[(m - 2, NT - 1)])
            ws = slice((m % 2) * 1024, (m % 2) * 1024 + 1024)
            S.vector.tensor_copy(whi[:, ws], wpq[:, ws])
            m_wsp[m] = S.vector.tensor_tensor(wlo[:, ws], wpq[:, ws],
                                              whi[:, ws], ALU.subtract).inc("vx", 1)
            if m == 0:
                for tok in lds:
                    S.tensor.wait(tok)
            S.tensor.wait("vx", m_wsp[m])
            for nt in range(NT):
                g = m * NT + nt
                if g >= 2:
                    S.tensor.wait("sc", m_ev[(m, nt - 2)] if nt >= 2
                                  else m_ev[(m - 1, nt + 2)])
                for c in range(DCH):
                    co = (m % 2) * 1024 + c * 128
                    for wop, cop in ((whi, chi), (wlo, chi), (whi, clo)):
                        r = S.tensor.matmul(
                            pp[g % 2][:], wop[:, co:co + 128],
                            cop[:, c * T + nt * 512:c * T + nt * 512 + 512],
                            start=(c == 0 and wop is whi and cop is chi),
                            stop=(c == DCH - 1 and cop is clo))
                m_mm[(m, nt)] = r.inc("mm", 1)
                S.scalar.wait("mm", m_mm[(m, nt)])
                if g >= 2:
                    S.scalar.wait("vx", m_res[(m, nt - 2)] if nt >= 2
                                  else m_res[(m - 1, nt + 2)])
                m_ev[(m, nt)] = S.scalar.activation(
                    p32[:, (g % 2) * 512:(g % 2) * 512 + 512], pp[g % 2][:],
                    AF.Identity, bias=bp_col[:, m:m + 1]).inc("sc", 1)
                S.vector.wait("sc", m_ev[(m, nt)])
                xs = x_sb[:, m * T + nt * 512:m * T + nt * 512 + 512]
                m_res[(m, nt)] = S.vector.tensor_add(
                    xs, xs, p32[:, (g % 2) * 512:(g % 2) * 512 + 512]).inc("vx", 1)
        S.flush()


def _sub_att0(p, cst, d_wkqv, d_wproj0, s_qkv, s_ctx, s_row128):
    nc = p.nc
    _att_phaseA(p, cst, d_wkqv, s_qkv, relu_rows=0)
    NH = 16
    with ExitStack() as st0:
        negBc = st0.enter_context(nc.sbuf_tensor(_uname("a0_negB"), [128, NH], F32))
        with ExitStack() as st:
            qb = st.enter_context(nc.sbuf_tensor(_uname("b0_qb"), [64, 2 * 2 * T], BF16))
            cm = st.enter_context(nc.sbuf_tensor(_uname("b0_cm"), [128, 1], F32))
            cmt = st.enter_context(nc.sbuf_tensor(_uname("b0_cmt"), [128, 1], F32))
            cmrow = st.enter_context(nc.sbuf_tensor(_uname("b0_cmrow"), [1, 128], F32))
            b64 = st.enter_context(nc.sbuf_tensor(_uname("b0_b64"), [1, 1], F32))
            mrow = st.enter_context(nc.sbuf_tensor(_uname("b0_mrow"), [1, NH], F32))
            ps_s0 = st.enter_context(nc.psum_tensor(_uname("b0_ps"), [128, T], F32))
            ps_t = st.enter_context(nc.psum_tensor(_uname("b0_pt"), [128, 128], F32))
            pb = st.enter_context(nc.psum_tensor(_uname("b0_pb"), [128, NH], F32))
            S = Sched(p)
            ng64 = S.gpsimd.memset(b64[:], 72.0).inc("gp", 1)
            S.scalar.wait("gp", ng64)
            m_ld = {}; m_mm = {}; m_red = {}; m_tr = {}; m_mx = {}
            for h in range(NH):
                hp = h % 2
                if h >= 2:
                    S.sync.wait("mm", m_mm[(h - 2, 15)])
                toks = []
                for j, rowbase in enumerate((1024 + 64 * h, 64 * h)):
                    toks.append(S.sync.dma_start(
                        out=qb[:, (hp * 2 + j) * T:(hp * 2 + j) * T + T],
                        in_=s_qkv[0][rowbase:rowbase + 64, :]).inc("qdma", 16))
                q_hi = qb[:, hp * 2 * T:hp * 2 * T + T]
                k_hi = qb[:, (hp * 2 + 1) * T:(hp * 2 + 1) * T + T]
                for tok in toks:
                    S.tensor.wait(tok)
                for jc in range(16):
                    i0 = 128 * jc
                    if h >= 1 or jc >= 1:
                        S.tensor.wait("vx", m_red[(h, jc - 1)] if jc >= 1
                                      else m_red[(h - 1, 15)])
                    for c in range(jc // 4, 4):
                        istart = max(i0, 512 * c)
                        ilen = 512 * (c + 1) - istart
                        r = S.tensor.matmul(
                            ps_s0[:, istart:istart + ilen],
                            k_hi[0:64, i0:i0 + 128],
                            q_hi[0:64, istart:istart + ilen],
                            start=True, stop=True)
                    m_mm[(h, jc)] = r.inc("mm", 1)
                    S.vector.wait("mm", m_mm[(h, jc)])
                    S.vector.tensor_add(ps_s0[:, i0:i0 + 128], ps_s0[:, i0:i0 + 128],
                                        cst["masksb"][:])
                    if jc == 0:
                        if h >= 1:
                            S.vector.wait(m_tr[h - 1])
                        m_red[(h, 0)] = S.vector.reduce_max(
                            cm[:], ps_s0[:, i0:T], axis=AXX).inc("vx", 1)
                    else:
                        nrd = S.vector.reduce_max(cmt[:], ps_s0[:, i0:T],
                                                  axis=AXX).inc("vx", 1)
                        S.vector.wait("vx", nrd)
                        m_red[(h, jc)] = S.vector.tensor_max(
                            cm[:], cm[:], cmt[:]).inc("vx", 1)
                S.sync.wait("vx", m_red[(h, 15)])
                if h >= 1:
                    S.sync.wait("sc", m_mx[h - 1])
                t1 = S.sync.dma_start(out=s_row128, in_=cm[:]).inc("xdma", 16)
                S.sync.wait(t1)
                ntr = S.sync.dma_start(out=cmrow[:],
                                       in_=s_row128.rearrange("p c -> c p")
                                       ).inc("xdma", 16)
                m_tr[h] = ntr
                S.vector.wait(ntr)
                nrm = S.vector.reduce_max(mrow[:, h:h + 1], cmrow[:],
                                          axis=AXX).inc("vx", 1)
                S.scalar.wait("vx", nrm)
                m_mx[h] = S.scalar.activation(
                    mrow[:, h:h + 1], mrow[:, h:h + 1], AF.Identity,
                    scale=-0.125, bias=b64[:]).inc("sc", 1)
            S.tensor.wait("sc", m_mx[NH - 1])
            for h in range(NH):
                r = S.tensor.matmul(pb[:, h:h + 1], cst["onesrow_f"][:],
                                    mrow[:, h:h + 1], start=True, stop=True)
            nmm = r.inc("mm", 1)
            S.scalar.wait("mm", nmm)
            nsc = S.scalar.copy(negBc[:], pb[:]).inc("sc", 1)
            S.gpsimd.wait("sc", nsc)
            S.flush()
        with ExitStack() as st:
            qkb = st.enter_context(nc.sbuf_tensor(_uname("b_qk"), [64, 2 * 4 * T], BF16))
            vt8 = st.enter_context(nc.sbuf_tensor(_uname("b_vt8"), [128, 2 * 16 * 512], BF16))
            vext = st.enter_context(nc.sbuf_tensor(_uname("b_vext"), [128, 2 * 16 * 130], BF16))
            ex32 = st.enter_context(nc.sbuf_tensor(_uname("b_ex32"), [128, 2 * 512], F32))
            ehi = st.enter_context(nc.sbuf_tensor(_uname("b_ehi"), [128, 2 * T], BF16))
            elo = st.enter_context(nc.sbuf_tensor(_uname("b_elo"), [128, 2 * T], BF16))
            c32 = st.enter_context(nc.sbuf_tensor(_uname("b_c32"), [64, 512], F32))
            cn32 = st.enter_context(nc.sbuf_tensor(_uname("b_cn32"), [64, 512], F32))
            chl_h = st.enter_context(nc.sbuf_tensor(_uname("b_chlh"), [64, 2 * T], BF16))
            chl_l = st.enter_context(nc.sbuf_tensor(_uname("b_chll"), [64, 2 * T], BF16))
            rr = st.enter_context(nc.sbuf_tensor(_uname("b_rr"), [1, 512], F32))
            rhl = st.enter_context(nc.sbuf_tensor(_uname("b_rhl"), [1, 2 * 512], BF16))
            psc0 = st.enter_context(nc.psum_tensor(_uname("b_psc0"), [128, 512], F32))
            psc1 = st.enter_context(nc.psum_tensor(_uname("b_psc1"), [128, 512], F32))
            ps_ctx = st.enter_context(nc.psum_tensor(_uname("b_pctx"), [65, T], F32))
            ps_rb = st.enter_context(nc.psum_tensor(_uname("b_prb"), [64, 512], F32))
            S = Sched(p)
            ps_sc = (psc0, psc1)
            for hp in range(2):
                for jc in range(16):
                    c0 = hp * 16 * 130 + jc * 130
                    S.gpsimd.memset(vext[:, c0 + 64:c0 + 65], 1.0)
                    S.gpsimd.memset(vext[:, c0 + 129:c0 + 130], 0.0)
            ngp = S.gpsimd.engine_nop().inc("gp", 1)
            m_vt = {}; m_cmm = {}; m_evdone = {}; m_store = {}
            m_exp = {}; m_esp = {}; m_msk = {}; m_evmul = {}; m_cseg = {}
            seg_id = 0
            for h in range(DEBUG.get("att0_nh", NH)):
                grp, hp, hh = h // 8, h % 2, h % 8
                if hh == 0:
                    if grp >= 1:
                        S.sync.wait("mm", m_cmm[grp * 8 - 1])
                    toks = []
                    for pl in range(2):
                        for jc in range(16):
                            toks.append(S.sync.dma_start_transpose(
                                vt8[:, (pl * 16 + jc) * 512:(pl * 16 + jc) * 512 + 512],
                                s_qkv[pl][2048 + 512 * grp:2048 + 512 * (grp + 1),
                                          128 * jc:128 * (jc + 1)]).inc("xdma", 16))
                    m_vt[grp] = toks[-NROT:]
                if h >= 2:
                    S.sync.wait("mm", m_cmm[h - 2])
                qtoks = []
                for j, rowbase in enumerate((1024 + 64 * h, 64 * h)):
                    for pl in range(2):
                        qtoks.append(S.sync.dma_start(
                            out=qkb[:, hp * 4 * T + (2 * j + pl) * T:
                                    hp * 4 * T + (2 * j + pl) * T + T],
                            in_=s_qkv[pl][rowbase:rowbase + 64, :]).inc("qdma", 16))
                S.vector.wait("gp", ngp)
                for tok in m_vt[grp]:
                    S.vector.wait(tok)
                if h >= 2:
                    S.vector.wait("mm", m_cmm[h - 2])
                for jc in range(16):
                    c0 = hp * 16 * 130 + jc * 130
                    S.vector.tensor_copy(vext[:, c0:c0 + 64],
                                         vt8[:, jc * 512 + hh * 64:jc * 512 + hh * 64 + 64])
                    r = S.vector.tensor_copy(
                        vext[:, c0 + 65:c0 + 129],
                        vt8[:, (16 + jc) * 512 + hh * 64:(16 + jc) * 512 + hh * 64 + 64])
                nvext = r.inc("vx", 1)
                QH = hp * 4 * T
                q_hi = qkb[:, QH + 0 * T:QH + 1 * T]
                q_lo = qkb[:, QH + 1 * T:QH + 2 * T]
                k_hi = qkb[:, QH + 2 * T:QH + 3 * T]
                k_lo = qkb[:, QH + 3 * T:QH + 4 * T]
                S.tensor.wait("vx", nvext)
                for tok in qtoks:
                    S.tensor.wait(tok)
                if h >= 1:
                    S.tensor.wait("vx", m_evdone[h - 1])
                for jc in range(16):
                    i0 = 128 * jc
                    kt_hi = k_hi[:, i0:i0 + 128]
                    kt_lo = k_lo[:, i0:i0 + 128]
                    for c in range(jc // 4, 4):
                        istart = max(i0, 512 * c)
                        ilen = 512 * (c + 1) - istart
                        sp = ps_sc[seg_id % 2]
                        if seg_id >= 2:
                            S.tensor.wait("sc", m_exp[seg_id - 2])
                        for a, b in ((kt_hi, q_hi), (kt_lo, q_hi), (kt_hi, q_lo)):
                            r = S.tensor.matmul(
                                sp[:, 0:ilen], a[0:64, :], b[0:64, istart:istart + ilen],
                                start=(a is kt_hi and b is q_hi),
                                stop=(b is q_lo))
                        nsmm = r.inc("mm", 1)
                        has_mask = (c == jc // 4)
                        if has_mask:
                            S.vector.wait("mm", nsmm)
                            m_msk[seg_id] = S.vector.tensor_add(
                                sp[:, 0:128], sp[:, 0:128], cst["masksb"][:]).inc("vx", 1)
                        if has_mask:
                            S.scalar.wait("vx", m_msk[seg_id])
                        else:
                            S.scalar.wait("mm", nsmm)
                        if seg_id >= 2:
                            S.scalar.wait("vx", m_esp[seg_id - 2])
                        epar = seg_id % 2
                        m_exp[seg_id] = S.scalar.activation(
                            ex32[:, epar * 512:epar * 512 + ilen], sp[:, 0:ilen],
                            AF.Exp, bias=negBc[:, h:h + 1], scale=0.125).inc("sc", 1)
                        S.vector.wait("sc", m_exp[seg_id])
                        if c == jc // 4:
                            if jc >= 2:
                                S.vector.wait("mm", m_cseg[(h, jc - 2, 3)])
                            elif h >= 1:
                                S.vector.wait("mm", m_cmm[h - 1])
                        jcp = jc % 2
                        es = slice(jcp * T + istart, jcp * T + istart + ilen)
                        S.vector.tensor_copy(ehi[:, es], ex32[:, epar * 512:epar * 512 + ilen])
                        m_esp[seg_id] = S.vector.tensor_tensor(
                            elo[:, es], ex32[:, epar * 512:epar * 512 + ilen],
                            ehi[:, es], ALU.subtract).inc("vx", 1)
                        S.tensor.wait("vx", m_esp[seg_id])
                        c0 = hp * 16 * 130 + jc * 130
                        for lhs_off, rhs_buf in ((0, ehi), (65, ehi), (0, elo)):
                            r = S.tensor.matmul(
                                ps_ctx[:, istart:istart + ilen],
                                vext[:, c0 + lhs_off:c0 + lhs_off + 65],
                                rhs_buf[:, es],
                                start=(jc == 0 and lhs_off == 0 and rhs_buf is ehi),
                                stop=(jc == 4 * c + 3 and rhs_buf is elo))
                        m_cseg[(h, jc, c)] = r.inc("mm", 1)
                        seg_id += 1
                m_cmm[h] = m_cseg[(h, 15, 3)]
                if h == DEBUG.get("att0_midh", -1) and "att0_midpctx" in DEBUG:
                    o_mid, dbgbuf = DEBUG.pop("att0_midpctx")
                    S.scalar.wait("mm", m_cmm[h])
                    nmd0 = S.scalar.copy(dbgbuf[0:64, 0:512], ps_ctx[0:64, 1536:2048]).inc("sc", 1)
                    S.vector.wait("mm", m_cmm[h])
                    nmd = S.vector.tensor_copy(dbgbuf[64:65, 0:512],
                                               ps_ctx[64:65, 1536:2048]).inc("vx", 1)
                    S.sync.wait("sc", nmd0)
                    S.sync.wait("vx", nmd)
                    nmdd = S.sync.dma_start(out=o_mid[0:64, :], in_=dbgbuf[0:64, 0:512]).inc("odma", 16)
                    nmdd2 = S.sync.dma_start(out=o_mid[64:65, :], in_=dbgbuf[64:65, 0:512]).inc("odma", 16)
                    S.scalar.wait(nmdd)
                    S.scalar.wait(nmdd2)
                for c in range(4):
                    S.scalar.wait("mm", m_cmm[h])
                    if c >= 1:
                        S.scalar.wait("vx", m_evmul[(h, c - 1)])
                    elif h >= 1:
                        S.scalar.wait("vx", m_evmul[(h - 1, 3)])
                    nsc = S.scalar.copy(c32[:], ps_ctx[0:64, 512 * c:512 * (c + 1)]).inc("sc", 1)
                    S.vector.wait("mm", m_cmm[h])
                    if h >= 2 and c == 0:
                        for tok in m_store[h - 2]:
                            S.vector.wait(tok)
                    nr0 = S.vector.reciprocal(
                        rr[:], ps_ctx[64:65, 512 * c:512 * (c + 1)]).inc("vx", 1)
                    S.vector.wait("vx", nr0)
                    nr1 = S.vector.tensor_copy(rhl[:, 0:512], rr[:]).inc("vx", 1)
                    S.vector.wait("vx", nr1)
                    nrl = S.vector.tensor_tensor(rhl[:, 512:1024], rr[:],
                                                 rhl[:, 0:512], ALU.subtract).inc("vx", 1)
                    S.tensor.wait("vx", nrl)
                    if c >= 1:
                        S.tensor.wait("vx", m_evmul[(h, c - 1)])
                    elif h >= 1:
                        S.tensor.wait("vx", m_evmul[(h - 1, 3)])
                    S.tensor.matmul(ps_rb[:], cst["onesrow_b"][0:1, 0:64],
                                    rhl[:, 0:512], start=True, stop=False)
                    r = S.tensor.matmul(ps_rb[:], cst["onesrow_b"][0:1, 0:64],
                                        rhl[:, 512:1024], start=False, stop=True)
                    nrb = r.inc("mm", 1)
                    S.vector.wait("mm", nrb)
                    S.vector.wait("sc", nsc)
                    S.vector.tensor_tensor(cn32[:], c32[:], ps_rb[0:64, :], ALU.mult)
                    S.vector.tensor_copy(chl_h[:, hp * T + 512 * c:hp * T + 512 * (c + 1)],
                                         cn32[:])
                    m_evmul[(h, c)] = S.vector.tensor_tensor(
                        chl_l[:, hp * T + 512 * c:hp * T + 512 * (c + 1)], cn32[:],
                        chl_h[:, hp * T + 512 * c:hp * T + 512 * (c + 1)],
                        ALU.subtract).inc("vx", 1)
                m_evdone[h] = m_evmul[(h, 3)]
                S.sync.wait("vx", m_evdone[h])
                st0_ = S.sync.dma_start(out=s_ctx[0][64 * h:64 * (h + 1), :],
                                        in_=chl_h[:, hp * T:hp * T + T]).inc("tdma", 16)
                st1_ = S.sync.dma_start(out=s_ctx[1][64 * h:64 * (h + 1), :],
                                        in_=chl_l[:, hp * T:hp * T + T]).inc("tdma", 16)
                m_store[h] = (st0_, st1_)
            for tok in p.dma_tokens("tdma"):
                S.gpsimd.wait(tok)
            S.flush()
            if "att0_vext" in DEBUG:
                o_ve = DEBUG.pop("att0_vext")
                S2 = Sched(p)
                nd = S2.sync.dma_start(out=o_ve[:, 0:4160], in_=vext[:]).inc("odma", 16)
                nd2 = S2.sync.dma_start(out=o_ve[:, 4160:4160 + 2 * T], in_=ehi[:]).inc("odma", 16)
                S2.sync.wait(nd)
                S2.sync.wait(nd2)
                S2.flush()
            if "att0_pctx" in DEBUG:
                o_pc = DEBUG.pop("att0_pctx")
                S = Sched(p)
                ns = S.scalar.copy(ex32[0:64, 0:512], ps_ctx[0:64, 0:512]).inc("sc", 1)
                S.vector.wait("sc", ns)
                nv = S.vector.tensor_copy(ex32[64:65, 512:1024],
                                          ps_ctx[64:65, 0:512]).inc("vx", 1)
                S.sync.wait("vx", nv)
                nd = S.sync.dma_start(out=o_pc[0:64, 0:512],
                                      in_=ex32[0:64, 0:512]).inc("odma", 16)
                nd2 = S.sync.dma_start(out=o_pc[64:65, 0:512],
                                       in_=ex32[64:65, 512:1024]).inc("odma", 16)
                S.sync.wait(nd)
                S.sync.wait(nd2)
                if "att0_negb" in DEBUG:
                    nb = S.sync.dma_start(out=DEBUG.pop("att0_negb"),
                                          in_=negBc[:]).inc("odma", 16)
                    S.sync.wait(nb)
                S.flush()
    _att_phaseC(p, cst, d_wproj0, cst["bp0c"], s_ctx)


def _sub_att1(p, cst, d_wqkv1, d_wproj1, s_qkv, s_ctx):
    nc = p.nc
    _att_phaseA(p, cst, d_wqkv1, s_qkv, relu_rows=16)
    NH, HD = 32, 32
    with ExitStack() as st:
        kt8 = st.enter_context(nc.sbuf_tensor(_uname("c_kt8"), [128, 2 * 16 * 512], BF16))
        vt8 = st.enter_context(nc.sbuf_tensor(_uname("c_vt8"), [128, 2 * 16 * 512], BF16))
        qb = st.enter_context(nc.sbuf_tensor(_uname("c_qb"), [32, 2 * 2 * T], BF16))
        l2_32 = st.enter_context(nc.sbuf_tensor(_uname("c_l232"), [32, 33], F32))
        l2h = st.enter_context(nc.sbuf_tensor(_uname("c_l2h"), [32, 2 * 33], BF16))
        l2l = st.enter_context(nc.sbuf_tensor(_uname("c_l2l"), [32, 2 * 33], BF16))
        ckrow = st.enter_context(nc.sbuf_tensor(_uname("c_ck"), [1, 32], F32))
        o32 = st.enter_context(nc.sbuf_tensor(_uname("c_o32"), [32, 512], F32))
        on32 = st.enter_context(nc.sbuf_tensor(_uname("c_on32"), [32, 512], F32))
        ohl_h = st.enter_context(nc.sbuf_tensor(_uname("c_ohlh"), [32, 2 * T], BF16))
        ohl_l = st.enter_context(nc.sbuf_tensor(_uname("c_ohll"), [32, 2 * T], BF16))
        rr = st.enter_context(nc.sbuf_tensor(_uname("c_rr"), [1, 512], F32))
        rhl = st.enter_context(nc.sbuf_tensor(_uname("c_rhl"), [1, 2 * 512], BF16))
        onescol_b = st.enter_context(nc.sbuf_tensor(_uname("c_ones"), [128, 1], BF16))
        ps_ck = st.enter_context(nc.psum_tensor(_uname("c_pck"), [1, 32], F32))
        ps_vk = st.enter_context(nc.psum_tensor(_uname("c_pvk"), [32, 32], F32))
        ps_ckT = st.enter_context(nc.psum_tensor(_uname("c_pckT"), [32, 1], F32))
        ps_o0 = st.enter_context(nc.psum_tensor(_uname("c_po0"), [33, 512], F32))
        ps_o1 = st.enter_context(nc.psum_tensor(_uname("c_po1"), [33, 512], F32))
        ps_rb = st.enter_context(nc.psum_tensor(_uname("c_prb"), [32, 512], F32))
        S = Sched(p)
        ps_o = (ps_o0, ps_o1)
        ng = S.gpsimd.memset(onescol_b[:], 1.0).inc("gp", 1)
        S.tensor.wait("gp", ng)
        m_vt = {}; m_omm = {}; m_store = {}; m_evmul = {}; m_vkev = {}; m_l2 = {}
        for h in range(NH):
            grp, hp, hh = h // 16, h % 2, h % 16
            if hh == 0:
                if grp >= 1:
                    S.sync.wait("mm", m_omm[(grp * 16 - 1, 3)])
                toks = []
                for dst, rowbase in ((kt8, 1024), (vt8, 2048)):
                    for pl in range(2):
                        for jc in range(16):
                            toks.append(S.sync.dma_start_transpose(
                                dst[:, (pl * 16 + jc) * 512:(pl * 16 + jc) * 512 + 512],
                                s_qkv[pl][rowbase + 512 * grp:rowbase + 512 * (grp + 1),
                                          128 * jc:128 * (jc + 1)]).inc("xdma", 16))
                m_vt[grp] = toks[-NROT:]
            if h >= 2:
                S.sync.wait("mm", m_omm[(h - 2, 3)])
            qtoks = []
            for pl in range(2):
                qtoks.append(S.sync.dma_start(
                    out=qb[:, (hp * 2 + pl) * T:(hp * 2 + pl) * T + T],
                    in_=s_qkv[pl][32 * h:32 * (h + 1), :]).inc("qdma", 16))
            for tok in m_vt[grp]:
                S.tensor.wait(tok)
            if h >= 1:
                S.tensor.wait("sc", m_vkev[h - 1])
            for pl in range(2):
                for jc in range(16):
                    r = S.tensor.matmul(
                        ps_ck[:], onescol_b[:],
                        kt8[:, (pl * 16 + jc) * 512 + hh * 32:(pl * 16 + jc) * 512 + hh * 32 + 32],
                        start=(pl == 0 and jc == 0), stop=(pl == 1 and jc == 15))
            for pl_l, pl_r in ((0, 0), (1, 0), (0, 1)):
                for jc in range(16):
                    ks = kt8[:, (pl_l * 16 + jc) * 512 + hh * 32:(pl_l * 16 + jc) * 512 + hh * 32 + 32]
                    vs = vt8[:, (pl_r * 16 + jc) * 512 + hh * 32:(pl_r * 16 + jc) * 512 + hh * 32 + 32]
                    r = S.tensor.matmul(ps_vk[:], ks, vs,
                                        start=(pl_l == 0 and pl_r == 0 and jc == 0),
                                        stop=(pl_l == 0 and pl_r == 1 and jc == 15))
            nvk = r.inc("mm", 1)
            S.scalar.wait("mm", nvk)
            if h >= 1:
                S.scalar.wait("vx", m_l2[h - 1])
            S.scalar.copy(ckrow[:], ps_ck[:])
            nsc = S.scalar.copy(l2_32[:, 0:32], ps_vk[:]).inc("sc", 1)
            S.tensor.wait("sc", nsc)
            r = S.tensor.matmul(ps_ckT[:], ckrow[:], cst["one_f"][:],
                                start=True, stop=True)
            nck = r.inc("mm", 1)
            S.scalar.wait("mm", nck)
            m_vkev[h] = S.scalar.copy(l2_32[:, 32:33], ps_ckT[:]).inc("sc", 1)
            l2s = slice(hp * 33, hp * 33 + 33)
            if h >= 2:
                S.scalar.wait("mm", m_omm[(h - 2, 3)])
            nhi2 = S.scalar.copy(l2h[:, l2s], l2_32[:]).inc("sc", 1)
            S.vector.wait("sc", nhi2)
            nl2 = S.vector.tensor_tensor(l2l[:, l2s], l2_32[:], l2h[:, l2s],
                                         ALU.subtract).inc("vx", 1)
            m_l2[h] = nl2
            S.tensor.wait("vx", nl2)
            for tok in qtoks:
                S.tensor.wait(tok)
            for it in range(NT):
                op = ps_o[it % 2]
                if it >= 2:
                    S.tensor.wait("vx", m_evmul[(h, it - 2)])
                elif h >= 1 and it < 2:
                    S.tensor.wait("vx", m_evmul[(h - 1, it + 2)])
                qhi = qb[:, hp * 2 * T + it * 512:hp * 2 * T + it * 512 + 512]
                qlo = qb[:, (hp * 2 + 1) * T + it * 512:(hp * 2 + 1) * T + it * 512 + 512]
                for lt, qt in ((l2h, qhi), (l2l, qhi), (l2h, qlo)):
                    r = S.tensor.matmul(op[:], lt[:, l2s], qt,
                                        start=(lt is l2h and qt is qhi),
                                        stop=(qt is qlo))
                m_omm[(h, it)] = r.inc("mm", 1)
                S.vector.wait("mm", m_omm[(h, it)])
                if h >= 2 and it == 0:
                    for tok in m_store[h - 2]:
                        S.vector.wait(tok)
                na0 = S.vector.tensor_scalar_add(rr[:], op[32:33, :], 1.0e-15).inc("vx", 1)
                S.vector.wait("vx", na0)
                na1 = S.vector.reciprocal(rr[:], rr[:]).inc("vx", 1)
                S.vector.wait("vx", na1)
                na2 = S.vector.tensor_copy(rhl[:, 0:512], rr[:]).inc("vx", 1)
                S.vector.wait("vx", na2)
                nrl = S.vector.tensor_tensor(rhl[:, 512:1024], rr[:],
                                             rhl[:, 0:512], ALU.subtract).inc("vx", 1)
                S.tensor.wait("vx", nrl)
                S.tensor.matmul(ps_rb[:], cst["onesrow_b"][0:1, 0:32],
                                rhl[:, 0:512], start=True, stop=False)
                r = S.tensor.matmul(ps_rb[:], cst["onesrow_b"][0:1, 0:32],
                                    rhl[:, 512:1024], start=False, stop=True)
                nrb = r.inc("mm", 1)
                S.scalar.wait("mm", m_omm[(h, it)])
                if it >= 1:
                    S.scalar.wait("vx", m_evmul[(h, it - 1)])
                elif h >= 1:
                    S.scalar.wait("vx", m_evmul[(h - 1, 3)])
                nsc = S.scalar.copy(o32[:], op[0:32, :]).inc("sc", 1)
                S.vector.wait("mm", nrb)
                S.vector.wait("sc", nsc)
                S.vector.tensor_tensor(on32[:], o32[:], ps_rb[:], ALU.mult)
                S.vector.tensor_copy(ohl_h[:, hp * T + it * 512:hp * T + (it + 1) * 512],
                                     on32[:])
                m_evmul[(h, it)] = S.vector.tensor_tensor(
                    ohl_l[:, hp * T + it * 512:hp * T + (it + 1) * 512], on32[:],
                    ohl_h[:, hp * T + it * 512:hp * T + (it + 1) * 512],
                    ALU.subtract).inc("vx", 1)
            S.sync.wait("vx", m_evmul[(h, 3)])
            st0_ = S.sync.dma_start(out=s_ctx[0][32 * h:32 * (h + 1), :],
                                    in_=ohl_h[:, hp * T:hp * T + T]).inc("tdma", 16)
            st1_ = S.sync.dma_start(out=s_ctx[1][32 * h:32 * (h + 1), :],
                                    in_=ohl_l[:, hp * T:hp * T + T]).inc("tdma", 16)
            m_store[h] = (st0_, st1_)
        for tok in p.dma_tokens("tdma"):
            S.gpsimd.wait(tok)
        S.flush()
    _att_phaseC(p, cst, d_wproj1, cst["bp1c"], s_ctx)


def _host_layouts(inputs):
    f32 = np.float32
    a = {}
    w1 = np.asarray(inputs["L0_ffn_w1"], f32)
    w2 = np.asarray(inputs["L0_ffn_w2"], f32)
    a["w1n"] = np.ascontiguousarray(w1)
    a["w2n"] = np.ascontiguousarray(w2)
    w1tb = w1.T.reshape(D, 32, 128).transpose(1, 0, 2)
    a["w1tb"] = np.ascontiguousarray(
        w1tb.reshape(32, 8, 128, 128).transpose(0, 2, 1, 3).reshape(32, 128, 1024))
    w2tb = w2.T.reshape(H0, 8, 128).transpose(1, 0, 2)
    a["w2tb"] = np.ascontiguousarray(
        w2tb.reshape(8, 32, 128, 128).transpose(0, 2, 1, 3).reshape(8, 128, 4096))
    a["b1"] = np.ascontiguousarray(np.asarray(inputs["L0_ffn_b1"], f32).reshape(1, H0))
    a["b2"] = np.ascontiguousarray(np.asarray(inputs["L0_ffn_b2"], f32).reshape(1, D))
    wkqv = np.asarray(inputs["L0_attn_w_kqv"], f32)
    wk = wkqv.reshape(16, 64, 3, D).transpose(2, 0, 1, 3).reshape(3 * D, D)
    WT, NB = wk.T, 24
    a["wkqvtb"] = np.ascontiguousarray(WT.reshape(D, NB, 128).transpose(1, 0, 2)
                                   .reshape(NB, 8, 128, 128).transpose(0, 2, 1, 3)
                                   .reshape(NB, 128, 1024))
    wp0 = np.asarray(inputs["L0_attn_w_proj"], f32)
    WT, NB = wp0.T, 8
    a["wproj0tb"] = np.ascontiguousarray(WT.reshape(D, NB, 128).transpose(1, 0, 2)
                                   .reshape(NB, 8, 128, 128).transpose(0, 2, 1, 3)
                                   .reshape(NB, 128, 1024))
    a["bproj0"] = np.ascontiguousarray(np.asarray(inputs["L0_attn_b_proj"], f32).reshape(1, D))
    wq1 = np.asarray(inputs["L1_attn_W_qkv"], f32)
    WT, NB = wq1.T, 24
    a["wqkv1tb"] = np.ascontiguousarray(WT.reshape(D, NB, 128).transpose(1, 0, 2)
                                   .reshape(NB, 8, 128, 128).transpose(0, 2, 1, 3)
                                   .reshape(NB, 128, 1024))
    wp1 = np.asarray(inputs["L1_attn_W_proj"], f32)
    WT, NB = wp1.T, 8
    a["wproj1tb"] = np.ascontiguousarray(WT.reshape(D, NB, 128).transpose(1, 0, 2)
                                   .reshape(NB, 8, 128, 128).transpose(0, 2, 1, 3)
                                   .reshape(NB, 128, 1024))
    a["bproj1"] = np.ascontiguousarray(np.asarray(inputs["L1_attn_b_proj"], f32).reshape(1, D))
    wg = np.asarray(inputs["L1_ffn_w_gate"], np.int32)
    wu = np.asarray(inputs["L1_ffn_w_up"], np.int32)
    wd = np.asarray(inputs["L1_ffn_w_down"], np.int32)
    wgtb = wg.T.reshape(D, 24, 128).transpose(1, 0, 2)
    a["wgtb"] = np.ascontiguousarray(
        wgtb.reshape(24, 8, 128, 128).transpose(0, 2, 1, 3).reshape(24, 128, 1024))
    wutb = wu.T.reshape(D, 24, 128).transpose(1, 0, 2)
    a["wutb"] = np.ascontiguousarray(
        wutb.reshape(24, 8, 128, 128).transpose(0, 2, 1, 3).reshape(24, 128, 1024))
    wdtb = wd.T.reshape(H1, 8, 128).transpose(1, 0, 2)
    a["wdtb"] = np.ascontiguousarray(
        wdtb.reshape(8, 24, 128, 128).transpose(0, 2, 1, 3).reshape(8, 128, 3072))
    a["ident_bf"] = np.eye(128, dtype=ml_dtypes.bfloat16)
    a["ident_f32"] = np.eye(128, dtype=f32)
    ii = np.arange(128)
    a["maskneg"] = np.where(ii[None, :] >= ii[:, None], 0.0, NEGBIG).astype(f32)
    return a


def kernel(**inputs):
    x = np.asarray(inputs["L0_attn_x"], np.float32)
    shared = _host_layouts(inputs)
    nc = build_nc(N_SUB_DEFAULT)
    in_maps = []
    for b in range(8):
        m = dict(shared)
        m["x_fm"] = np.ascontiguousarray(x[b].T)
        in_maps.append(m)
    res = run_bass_kernel_spmd(nc, in_maps, core_ids=list(range(8)))
    out = np.empty((8, T, D), np.float32)
    for b in range(8):
        out[b] = res.results[b]["y_fm"].T
    return out
